# version 04
# baseline (speedup 1.0000x reference)
import os
import numpy as np
import ml_dtypes
from contextlib import ExitStack

KPH = int(os.environ.get("KPH", "9"))
KP5 = int(os.environ.get("KP5", "9"))
KSAFE = int(os.environ.get("KSAFE", "0"))
KFENCE = int(os.environ.get("KFENCE", "15"))

import concourse.bass as bass
import concourse.mybir as mybir
from concourse.bass_utils import run_bass_kernel_spmd

F32 = mybir.dt.float32
BF16 = mybir.dt.bfloat16
U8 = mybir.dt.uint8
I32 = mybir.dt.int32
FP16 = mybir.dt.float16
FP8E5 = mybir.dt.float8e5
AF = mybir.ActivationFunctionType
ALU = mybir.AluOpType
SZ = {F32: 4, BF16: 2, U8: 1, I32: 4, FP8E5: 1, FP16: 2}

D = 2048
SEQ = 2048
NH = 16
NT = D // 128
NIT = 44
II = 5632
INVSQ = float(1.0 / np.sqrt(128.0))
SIGNM = -2139062144
M0E = 0x0E0E0E0E

ROLE_CFG = [
    dict(q0=0, qown=1152, cover=1152, qch=[(0, 512), (512, 512), (1024, 128)]),
    dict(q0=1152, qown=896, cover=2048, qch=[(1152, 512), (1664, 384)]),
]

ARENA_B = 210944
OC = 0
OX = 2048
OZ = 67584
OP = 104448
P1 = 65536


def halfperm(n):
    nb = n // 256
    pi = np.empty(n, np.int64)
    ar = 2 * np.arange(128)
    for c in range(nb):
        pi[256 * c:256 * c + 128] = 256 * c + ar
        pi[256 * c + 128:256 * c + 256] = 256 * c + ar + 1
    return pi


PI_D = halfperm(D)
PI_I = halfperm(II)

ENGS = ("sync", "vector", "scalar", "tensor", "gpsimd")


class Sem:
    def __init__(self, handle, name, init=0):
        self.h = handle
        self.name = name
        self.n = init


class Plan:
    def __init__(self, sems):
        self.streams = {e: [] for e in ENGS}
        self.sems = sems
        self.last_wait = {}

    def wait(self, eng, sem, val=None):
        v = sem.n if val is None else val
        if v <= 0:
            return
        key = (eng, sem.name)
        if self.last_wait.get(key, -1) >= v:
            return
        self.last_wait[key] = v
        self.streams[eng].append(lambda E, h=sem.h, v=v: E.wait_ge(h, v))

    def op(self, eng, fn, incs=()):
        caps = []
        for s, k in incs:
            s.n += k
            caps.append((s.h, k))

        def f(E, fn=fn, caps=caps):
            inst = fn(E)
            for h, k in caps:
                inst = inst.then_inc(h, k)
        self.streams[eng].append(f)

    def barrier_waits(self, engs=None):
        for e in (engs or ENGS):
            for s in self.sems:
                self.wait(e, s)


class Res:
    def __init__(self):
        self.free_at = {}

    def reader(self, sem, val):
        self.free_at[sem] = max(self.free_at.get(sem, 0), val)

    def wait_free(self, plan, eng):
        for s, v in self.free_at.items():
            plan.wait(eng, s, v)


def build_program():
    nc = bass.Bass("TRN2", target_bir_lowering=False)

    role_d = nc.dram_tensor("role_in", [1, 1], I32, kind="ExternalInput").ap()
    xT = nc.dram_tensor("xT", [D, SEQ], F32, kind="ExternalInput").ap()
    wq_t = nc.dram_tensor("wq_t", [D, D], F32, kind="ExternalInput").ap()
    wk_t = nc.dram_tensor("wk_t", [D, D], F32, kind="ExternalInput").ap()
    wv_t = nc.dram_tensor("wv_t", [D, D], F32, kind="ExternalInput").ap()
    wo_t = nc.dram_tensor("wo_t", [D, D], F32, kind="ExternalInput").ap()
    gu_b = nc.dram_tensor("gu_b", [D // 2, 2 * II], U8, kind="ExternalInput").ap()
    gu_s = nc.dram_tensor("gu_s", [128, 8 * 2 * II], FP16,
                          kind="ExternalInput").ap()
    dn_b = nc.dram_tensor("dn_b", [II // 2, D], U8, kind="ExternalInput").ap()
    dn_s = nc.dram_tensor("dn_s", [128, 22 * D], FP16,
                          kind="ExternalInput").ap()
    outp = nc.dram_tensor("outp", [D, 1152], F32, kind="ExternalOutput").ap()
    ybounce = nc.dram_tensor("ybounce", [D, 1152], F32, kind="Internal").ap()

    mask_np = (np.arange(896)[None, :] >= (np.arange(128)[:, None] + 384))
    mask_c = nc.inline_tensor(np.ascontiguousarray(
        mask_np.astype(ml_dtypes.bfloat16)), name="mask_c")
    onesc_c = nc.inline_tensor(np.ones((128, 1), ml_dtypes.bfloat16),
                               name="onesc_c")
    onesr_c = nc.inline_tensor(np.ones((1, 128), np.float32), name="onesr_c")

    es = ExitStack()
    with es:
        arena_t = es.enter_context(nc.sbuf_tensor("arena", [128, ARENA_B], U8))
        role_sb = es.enter_context(nc.sbuf_tensor("role_sb", [1, 1], I32))
        onesc_sb = es.enter_context(nc.sbuf_tensor("onesc", [128, 1], BF16))
        onesr_sb = es.enter_context(nc.sbuf_tensor("onesr", [1, 128], F32))
        eps_sb = es.enter_context(nc.sbuf_tensor("eps", [1, 2], F32))
        ps = [es.enter_context(nc.psum_tensor(f"ps{i}", [128, 512], F32))
              for i in range(8)]
        LDS_h = es.enter_context(nc.semaphore("LDS"))
        LDG_h = es.enter_context(nc.semaphore("LDG"))
        ST_h = es.enter_context(nc.semaphore("ST"))
        PE_h = es.enter_context(nc.semaphore("PE"))
        DV_h = es.enter_context(nc.semaphore("DV"))
        SC_h = es.enter_context(nc.semaphore("SC"))
        block = es.enter_context(nc.Block())

        arena = arena_t[:]

        def carve(off, nel, dt):
            return arena[:, off:off + nel * SZ[dt]].bitcast(dt)

        N_PRE = 4 * 16

        def make_plan(role):
            cfg = ROLE_CFG[role]
            q0, qown, cover = cfg["q0"], cfg["qown"], cfg["cover"]
            qch = cfg["qch"]
            qch_l = [(s - q0, w) for s, w in qch]
            npt = cover // 128

            LDS = Sem(LDS_h, "LDS", N_PRE)
            LDG = Sem(LDG_h, "LDG", 0)
            ST = Sem(ST_h, "ST", 0)
            PE = Sem(PE_h, "PE", 0)
            DV = Sem(DV_h, "DV", 1)
            SC = Sem(SC_h, "SC", 0)
            P = Plan([LDS, LDG, ST, PE, DV, SC])
            psr = [Res() for _ in range(8)]

            mask_sb = carve(OC, 896, BF16)
            xv = carve(OX, NT * SEQ, BF16)

            def xs(t, c0, w):
                return xv[:, t * SEQ + c0: t * SEQ + c0 + w]

            xnc = carve(OX, NT * 1152, BF16)

            def xn(t, l0, w):
                return xnc[:, t * 1152 + l0: t * 1152 + l0 + w]

            zv = carve(OZ, NH * 1152, BF16)

            def zsl(h, l0, w):
                return zv[:, h * 1152 + l0: h * 1152 + l0 + w]

            b1 = OP + P1
            wqv = carve(b1 + 0, NT * 128, BF16)
            wkv = carve(b1 + 4096, NT * 128, BF16)
            wvv = carve(b1 + 8192, NT * 128, BF16)
            qv = carve(b1 + 12288, 1152, BF16)
            kv = carve(b1 + 14592, 2048, BF16)
            vv = carve(b1 + 18688, 2048, BF16)
            atv = [carve(b1 + 22784 + 1024 * i, 512, BF16) for i in range(4)]
            bc1 = carve(b1 + 26880, 512, F32)
            row1 = carve(b1 + 28928, 512, F32)

            wov = carve(OP + 0, NT * D, BF16)
            y1v = carve(OP + 65536, NT * 512, F32)
            y1bf = carve(OP + 98304, 512, BF16)
            sqbf = carve(OP + 99328, 512, BF16)
            tmp2 = carve(OP + 100352, 512, F32)
            row2 = carve(OP + 102400, 512, F32)
            row2b = carve(OP + 104448, 512, F32)

            XSS = OX + 36864
            hbuf = carve(OP + 0, NIT * 1152, BF16)

            def hs(k, l0, w):
                return hbuf[:, k * 1152 + l0: k * 1152 + l0 + w]

            wb = [carve(OZ + 2048 * i, 1024, BF16) for i in range(16)]
            gby = [carve(OZ + 32768 + 1024 * i, 1024, U8) for i in range(2)]
            geo = [carve(OZ + 34816 + 1024 * i, 256, I32) for i in range(2)]
            sexp = carve(XSS + 0, 8192, BF16)
            silu_sb = [carve(XSS + 16384 + 2048 * i, 512, F32)
                       for i in range(2)]
            goo = [carve(XSS + 20480 + 1024 * i, 256, I32) for i in range(2)]
            gbt = carve(XSS + 22528, 256, I32)
            stage = carve(OP + 65536, 1152, BF16)

            wde = carve(OZ + 0, 5632, BF16)
            wdo = carve(OZ + 11264, 5632, BF16)
            dsc = carve(OZ + 22528, 5632, BF16)
            dby = carve(XSS + 0, 5632, U8)
            dei = carve(XSS + 5632, 1408, I32)
            doi = carve(XSS + 11264, 1408, I32)
            ytl = [carve(XSS + 16896 + 2048 * i, 512, F32) for i in range(2)]
            ysq = carve(XSS + 20992, 512, BF16)

            x3v = carve(OP + 0, NT * 512, F32)
            ybk = [carve(OP + 32768 + 2048 * i, 512, F32) for i in range(2)]
            otl = [carve(OP + 36864 + 2048 * i, 512, F32) for i in range(2)]
            x3bf = carve(OP + 40960, 512, BF16)
            x3sq = carve(OP + 41984, 512, BF16)
            tmp5 = carve(OP + 43008, 512, F32)
            row5 = carve(OP + 45056, 512, F32)
            row5b = carve(OP + 47104, 512, F32)

            rW = [Res() for _ in range(3)]
            rAT = [Res() for _ in range(4)]
            rWB = Res()
            rGBY = [Res() for _ in range(2)]
            rSEXP = Res()
            rDOWN = Res()
            rYT = [Res() for _ in range(2)]
            rYB = [Res() for _ in range(2)]
            rOT = [Res() for _ in range(2)]
            rSIL = [Res() for _ in range(2)]
            fdv = carve(OC + 1856, 16, F32)
            fsc = carve(OC + 1920, 16, F32)

            def fence_dv():
                P.op("vector", lambda E: E.tensor_copy(
                    fdv[0:1, 8:16], fdv[0:1, 0:8]), [(DV, 1)])

            def fence_sc():
                P.op("scalar", lambda E: E.copy(
                    fsc[0:1, 8:16], fsc[0:1, 0:8]), [(SC, 1)])

            rY1BF = Res()
            rSQBF = Res()
            rYSQ = Res()
            rX3BF = Res()
            rX3SQ = Res()

            for g4 in range(4):
                xdst = xv.rearrange("p (t c) -> p t c", c=SEQ)[
                    :, 4 * g4:4 * g4 + 4, 0:cover]
                xsrc = xT.rearrange("(t p) c -> p t c", p=128)[
                    :, 4 * g4:4 * g4 + 4, 0:cover]
                P.op("gpsimd", lambda E, o=xdst, i=xsrc:
                     E.dma_start(out=o, in_=i), [(LDG, 16)])

            for h in range(NH if KPH >= 1 else 0):
                for wi, (sbv, dr) in enumerate(
                        [(wqv, wq_t), (wkv, wk_t), (wvv, wv_t)]):
                    rW[wi].wait_free(P, "gpsimd")
                    o3 = sbv.rearrange("p (t f) -> p t f", f=128)
                    i3 = dr.rearrange("(t p) f -> p t f", p=128)[
                        :, :, h * 128:(h + 1) * 128]
                    P.op("gpsimd", lambda E, o=o3, i=i3:
                         E.dma_start(out=o, in_=i), [(LDG, 16)])
                P.wait("tensor", LDG)

                wq3 = wqv.rearrange("p (t f) -> p t f", f=128)
                wk3 = wkv.rearrange("p (t f) -> p t f", f=128)
                wv3 = wvv.rearrange("p (t f) -> p t f", f=128)

                proj_jobs = [("q", l0, w) for (l0, w) in qch_l]
                c0 = 0
                while c0 < cover:
                    w = min(512, cover - c0)
                    proj_jobs.append(("k", c0, w))
                    c0 += w
                for kind, c0, w in proj_jobs:
                    psr[0].wait_free(P, "tensor")
                    wmat = wq3 if kind == "q" else wk3
                    xc0 = (q0 + c0) if kind == "q" else c0
                    for t in range(NT):
                        P.op("tensor", lambda E, t=t, xc0=xc0, w=w, wmat=wmat:
                             E.matmul(ps[0][:, 0:w], wmat[:, t, :],
                                      xs(t, xc0, w),
                                      start=(t == 0), stop=(t == NT - 1)),
                             [(PE, 1)] if t == NT - 1 else [])
                    P.wait("scalar", PE)
                    dst = qv if kind == "q" else kv
                    P.op("scalar", lambda E, c0=c0, w=w, dst=dst: E.activation(
                        dst[:, c0:c0 + w], ps[0][:, 0:w], AF.Copy), [(SC, 1)])
                    psr[0].reader(SC, SC.n)
                    if KFENCE & 1:
                        fence_sc()
                for pt in range(npt):
                    psr[0].wait_free(P, "tensor")
                    for t in range(NT):
                        P.op("tensor", lambda E, t=t, pt=pt: E.matmul(
                            ps[0][:, 0:128], xs(t, pt * 128, 128),
                            wv3[:, t, :],
                            start=(t == 0), stop=(t == NT - 1)),
                            [(PE, 1)] if t == NT - 1 else [])
                    P.wait("scalar", PE)
                    P.op("scalar", lambda E, pt=pt: E.activation(
                        vv[:, pt * 128:pt * 128 + 128], ps[0][:, 0:128],
                        AF.Copy), [(SC, 1)])
                    psr[0].reader(SC, SC.n)
                    if KFENCE & 1:
                        fence_sc()

                for (qg0, w), (l0, _w2) in zip(qch, qch_l):
                    pmax = (qg0 + w - 1) // 128
                    P.wait("tensor", SC)
                    for pt in range(pmax + 1):
                        pss = 1 + (pt & 1)
                        slot = pt % 4
                        psr[pss].wait_free(P, "tensor")
                        P.op("tensor", lambda E, pt=pt, pss=pss, l0=l0, w=w:
                             E.matmul(ps[pss][:, 0:w],
                                      kv[:, pt * 128:pt * 128 + 128],
                                      qv[:, l0:l0 + w], start=True, stop=True,
                                      skip_group_check=True), [(PE, 1)])
                        rAT[slot].wait_free(P, "scalar")
                        P.wait("scalar", PE)
                        P.op("scalar", lambda E, pss=pss, slot=slot, w=w:
                             E.activation(atv[slot][:, 0:w], ps[pss][:, 0:w],
                                          AF.Exp, scale=INVSQ), [(SC, 1)])
                        psr[pss].reader(SC, SC.n)
                        if KFENCE & 1:
                            fence_sc()
                        delta = qg0 - 128 * pt
                        if delta < 128:
                            P.wait("vector", SC)
                            P.op("vector", lambda E, slot=slot, w=w, d=delta:
                                 E.tensor_tensor(
                                     atv[slot][:, 0:w], atv[slot][:, 0:w],
                                     mask_sb[:, 384 + d:384 + d + w],
                                     ALU.mult), [(DV, 1)])
                            if KFENCE & 1:
                                fence_dv()
                            P.wait("tensor", DV)
                        else:
                            P.wait("tensor", SC)
                        if pt == 0:
                            psr[3].wait_free(P, "tensor")
                            psr[4].wait_free(P, "tensor")
                        P.op("tensor", lambda E, pt=pt, slot=slot, w=w,
                             pmax=pmax: E.matmul(
                                 ps[3][:, 0:w],
                                 vv[:, pt * 128:pt * 128 + 128],
                                 atv[slot][:, 0:w],
                                 start=(pt == 0), stop=(pt == pmax),
                                 skip_group_check=True),
                             [(PE, 1)] if pt == pmax else [])
                        P.op("tensor", lambda E, pt=pt, slot=slot, w=w,
                             pmax=pmax: E.matmul(
                                 ps[4][0:1, 0:w], onesc_sb[:],
                                 atv[slot][:, 0:w],
                                 start=(pt == 0), stop=(pt == pmax),
                                 skip_group_check=True),
                             [(PE, 1)] if pt == pmax else [])
                        rAT[slot].reader(PE, PE.n + 1)
                    P.wait("vector", PE)
                    P.op("vector", lambda E, w=w: E.tensor_copy(
                        bc1[0:1, 0:w], ps[4][0:1, 0:w]), [(DV, 1)])
                    psr[4].reader(DV, DV.n)
                    P.op("vector", lambda E, w=w: E.reciprocal(
                        row1[0:1, 0:w], bc1[0:1, 0:w]), [(DV, 1)])
                    if KFENCE & 1:
                        fence_dv()
                    psr[5].wait_free(P, "tensor")
                    P.wait("tensor", DV)
                    P.op("tensor", lambda E, w=w: E.matmul(
                        ps[5][:, 0:w], onesr_sb[:], row1[0:1, 0:w],
                        start=True, stop=True, skip_group_check=True),
                        [(PE, 1)])
                    P.wait("vector", PE)
                    P.op("vector", lambda E, w=w: E.tensor_copy(
                        bc1[:, 0:w], ps[5][:, 0:w]), [(DV, 1)])
                    psr[5].reader(DV, DV.n)
                    P.op("vector", lambda E, h=h, l0=l0, w=w: E.tensor_tensor(
                        zsl(h, l0, w), ps[3][:, 0:w], bc1[:, 0:w], ALU.mult),
                        [(DV, 1)])
                    psr[3].reader(DV, DV.n)
                    if KFENCE & 1:
                        fence_dv()
                    if KSAFE & 2:
                        P.barrier_waits()
                for wi in range(3):
                    rW[wi].reader(PE, PE.n)

            for g4 in range(4 if KPH >= 2 else 0):
                wod = wov.rearrange("p (t f) -> p t f", f=D)[
                    :, 4 * g4:4 * g4 + 4, :]
                wos = wo_t.rearrange("(t p) f -> p t f", p=128)[
                    :, 4 * g4:4 * g4 + 4, :]
                P.op("gpsimd", lambda E, o=wod, i=wos:
                     E.dma_start(out=o, in_=i), [(LDG, 16)])
            P.wait("tensor", LDG)
            P.barrier_waits(("tensor", "vector", "scalar"))
            wo3s = wov.rearrange("p (t f) -> p t f", f=D)

            for (qg0, w), (l0, _w2) in zip(qch if KPH >= 2 else [], qch_l):
                psr[6].wait_free(P, "tensor")
                psr[7].wait_free(P, "tensor")
                for dt in range(NT):
                    pso = dt & 1
                    psr[pso].wait_free(P, "tensor")
                    for ft in range(NH):
                        P.op("tensor", lambda E, dt=dt, ft=ft, l0=l0, w=w,
                             pso=pso: E.matmul(
                                 ps[pso][:, 0:w],
                                 wo3s[:, ft, dt * 128:dt * 128 + 128],
                                 zsl(ft, l0, w),
                                 start=(ft == 0), stop=(ft == NH - 1),
                                 skip_group_check=True),
                             [(PE, 1)] if ft == NH - 1 else [])
                    P.wait("vector", PE)
                    P.op("vector", lambda E, dt=dt, l0=l0, w=w, pso=pso:
                         E.tensor_tensor(y1v[:, dt * 512:dt * 512 + w],
                                         ps[pso][:, 0:w],
                                         xs(dt, q0 + l0, w), ALU.add),
                         [(DV, 1)])
                    psr[pso].reader(DV, DV.n)
                    rY1BF.wait_free(P, "vector")
                    P.op("vector", lambda E, dt=dt, w=w: E.tensor_copy(
                        y1bf[:, 0:w], y1v[:, dt * 512:dt * 512 + w]),
                        [(DV, 1)])
                    if KFENCE & 2:
                        fence_dv()
                    rSQBF.wait_free(P, "scalar")
                    P.wait("scalar", DV)
                    P.op("scalar", lambda E, dt=dt, w=w: E.activation(
                        sqbf[:, 0:w], y1v[:, dt * 512:dt * 512 + w],
                        AF.Square), [(SC, 1)])
                    if KFENCE & 2:
                        fence_sc()
                    P.wait("tensor", DV)
                    P.wait("tensor", SC)
                    P.op("tensor", lambda E, dt=dt, w=w: E.matmul(
                        ps[6][0:1, 0:w], onesc_sb[:], y1bf[:, 0:w],
                        start=(dt == 0), stop=(dt == NT - 1),
                        skip_group_check=True),
                        [(PE, 1)] if dt == NT - 1 else [])
                    P.op("tensor", lambda E, dt=dt, w=w: E.matmul(
                        ps[7][0:1, 0:w], onesc_sb[:], sqbf[:, 0:w],
                        start=(dt == 0), stop=(dt == NT - 1),
                        skip_group_check=True),
                        [(PE, 1)] if dt == NT - 1 else [])
                    rY1BF.reader(PE, PE.n + 1)
                    rSQBF.reader(PE, PE.n + 1)
                P.wait("vector", PE)
                P.op("vector", lambda E, w=w: E.tensor_scalar(
                    row2[0:1, 0:w], ps[6][0:1, 0:w], 1.0 / D, None, ALU.mult),
                    [(DV, 1)])
                P.op("vector", lambda E, w=w: E.tensor_scalar(
                    row2b[0:1, 0:w], ps[7][0:1, 0:w], 1.0 / D, None,
                    ALU.mult), [(DV, 1)])
                psr[6].reader(DV, DV.n)
                psr[7].reader(DV, DV.n)
                P.op("vector", lambda E, w=w: E.tensor_tensor(
                    tmp2[0:1, 0:w], row2[0:1, 0:w], row2[0:1, 0:w],
                    ALU.mult), [(DV, 1)])
                P.op("vector", lambda E, w=w: E.tensor_tensor(
                    row2b[0:1, 0:w], row2b[0:1, 0:w], tmp2[0:1, 0:w],
                    ALU.subtract), [(DV, 1)])
                if KFENCE & 2:
                    fence_dv()
                P.wait("scalar", DV)
                P.op("scalar", lambda E, w=w: E.activation(
                    row2b[0:1, 0:w], row2b[0:1, 0:w], AF.Ln,
                    bias=eps_sb[0:1, 0:1]), [(SC, 1)])
                P.op("scalar", lambda E, w=w: E.activation(
                    row2b[0:1, 0:w], row2b[0:1, 0:w], AF.Exp, scale=-0.5),
                    [(SC, 1)])
                if KFENCE & 2:
                    fence_sc()
                psr[2].wait_free(P, "tensor")
                psr[3].wait_free(P, "tensor")
                P.wait("tensor", DV)
                P.wait("tensor", SC)
                P.op("tensor", lambda E, w=w: E.matmul(
                    ps[2][:, 0:w], onesr_sb[:], row2[0:1, 0:w],
                    start=True, stop=True, skip_group_check=True), [(PE, 1)])
                P.op("tensor", lambda E, w=w: E.matmul(
                    ps[3][:, 0:w], onesr_sb[:], row2b[0:1, 0:w],
                    start=True, stop=True, skip_group_check=True), [(PE, 1)])
                P.wait("vector", PE)
                for dt in range(NT):
                    P.op("vector", lambda E, dt=dt, w=w: E.tensor_tensor(
                        tmp2[:, 0:w], y1v[:, dt * 512:dt * 512 + w],
                        ps[2][:, 0:w], ALU.subtract), [(DV, 1)])
                    P.op("vector", lambda E, dt=dt, l0=l0, w=w:
                         E.tensor_tensor(xs(dt, q0 + l0, w), tmp2[:, 0:w],
                                         ps[3][:, 0:w], ALU.mult), [(DV, 1)])
                psr[2].reader(DV, DV.n)
                psr[3].reader(DV, DV.n)
                if KSAFE & 1:
                    P.barrier_waits()

            for t in range(NT if KPH >= 2 else 0):
                P.op("vector", lambda E, t=t: E.tensor_copy(
                    stage[:, 0:qown], xs(t, q0, qown)), [(DV, 1)])
                P.op("vector", lambda E, t=t: E.tensor_copy(
                    xnc[:, 1152 * t:1152 * t + qown], stage[:, 0:qown]),
                    [(DV, 1)])
            if KFENCE & 4:
                fence_dv()

            P.barrier_waits()
            sexp3 = sexp.rearrange("p (g c) -> p g c", c=1024)
            for c in range(11 if KPH >= 3 else 0):
                rSEXP.wait_free(P, "gpsimd")
                src = gu_s.rearrange("p (g o) -> p g o", o=2 * II)[
                    :, :, 1024 * c:1024 * c + 1024]
                P.op("gpsimd", lambda E, src=src: E.dma_start(
                    out=sexp3, in_=src), [(LDG, 16)])
                sexp_ready = LDG.n
                for G in range(8):
                    gs = G & 1
                    rGBY[gs].wait_free(P, "sync")
                    P.op("sync", lambda E, G=G, c=c, gs=gs: E.dma_start(
                        out=gby[gs][:], in_=gu_b[128 * G:128 * G + 128,
                                                 1024 * c:1024 * c + 1024]),
                        [(LDS, 16)])
                    by_ready = LDS.n
                    bi = gby[gs].bitcast(I32)
                    ei, oi, ti = geo[gs], goo[gs], gbt
                    P.wait("vector", LDS, by_ready)
                    if G == 0:
                        rWB.wait_free(P, "vector")
                    P.op("vector", lambda E, bi=bi, ei=ei: E.tensor_scalar(
                        ei[:], bi[:], 4, SIGNM,
                        ALU.logical_shift_left, ALU.bitwise_and), [])
                    P.op("vector", lambda E, bi=bi, ti=ti: E.tensor_scalar(
                        ti[:], bi[:], 1, M0E,
                        ALU.logical_shift_left, ALU.bitwise_and), [])
                    P.op("vector", lambda E, ei=ei, ti=ti: E.tensor_tensor(
                        ei[:], ei[:], ti[:], ALU.bitwise_or), [])
                    P.op("vector", lambda E, bi=bi, oi=oi: E.tensor_scalar(
                        oi[:], bi[:], 3, M0E,
                        ALU.logical_shift_right, ALU.bitwise_and), [])
                    P.op("vector", lambda E, bi=bi, ti=ti: E.tensor_scalar(
                        ti[:], bi[:], SIGNM, None, ALU.bitwise_and), [])
                    P.op("vector", lambda E, oi=oi, ti=ti: E.tensor_tensor(
                        oi[:], oi[:], ti[:], ALU.bitwise_or), [(DV, 1)])
                    rGBY[gs].reader(DV, DV.n)
                    P.wait("vector", LDG, sexp_ready)
                    P.op("vector", lambda E, G=G, e8=ei.bitcast(FP8E5):
                         E.tensor_tensor(
                             wb[2 * G][:], e8[:],
                             sexp[:, G * 1024:G * 1024 + 1024], ALU.mult),
                         [(DV, 1)])
                    P.op("vector", lambda E, G=G, o8=oi.bitcast(FP8E5):
                         E.tensor_tensor(
                             wb[2 * G + 1][:], o8[:],
                             sexp[:, G * 1024:G * 1024 + 1024], ALU.mult),
                         [(DV, 1)])
                rSEXP.reader(DV, DV.n)
                if KFENCE & 4:
                    fence_dv()
                P.wait("tensor", DV)
                for pr in range(4):
                    kk = 4 * c + pr
                    for (l0, w) in qch_l:
                        psr[0].wait_free(P, "tensor")
                        psr[1].wait_free(P, "tensor")
                        for t in range(NT):
                            P.op("tensor", lambda E, t=t, pr=pr, l0=l0, w=w:
                                 E.matmul(
                                     ps[0][:, 0:w],
                                     wb[t][:, 256 * pr:256 * pr + 128],
                                     xn(t, l0, w),
                                     start=(t == 0), stop=(t == NT - 1),
                                     skip_group_check=True),
                                 [(PE, 1)] if t == NT - 1 else [])
                        for t in range(NT):
                            P.op("tensor", lambda E, t=t, pr=pr, l0=l0, w=w:
                                 E.matmul(
                                     ps[1][:, 0:w],
                                     wb[t][:, 256 * pr + 128:256 * pr + 256],
                                     xn(t, l0, w),
                                     start=(t == 0), stop=(t == NT - 1),
                                     skip_group_check=True),
                                 [(PE, 1)] if t == NT - 1 else [])
                        ssl = pr & 1
                        rSIL[ssl].wait_free(P, "scalar")
                        P.wait("scalar", PE)
                        P.op("scalar", lambda E, w=w, ssl=ssl: E.activation(
                            silu_sb[ssl][:, 0:w], ps[0][:, 0:w], AF.Silu,
                            scale=16384.0), [(SC, 1)])
                        if KFENCE & 4:
                            fence_sc()
                        psr[0].reader(SC, SC.n)
                        P.wait("vector", SC)
                        P.op("vector", lambda E, kk=kk, l0=l0, w=w, ssl=ssl:
                             E.tensor_tensor(hs(kk, l0, w),
                                             silu_sb[ssl][:, 0:w],
                                             ps[1][:, 0:w], ALU.mult),
                             [(DV, 1)])
                        psr[1].reader(DV, DV.n)
                        rSIL[ssl].reader(DV, DV.n)
                rWB.reader(PE, PE.n)

            P.barrier_waits()
            dsc3 = dsc.rearrange("p (g c) -> p g c", c=256)
            nqc = len(qch_l)
            for qi in range(nqc if KPH >= 4 else 0):
                psr[5 + qi].wait_free(P, "tensor")
            for k in range(8 if KPH >= 4 else 0):
                rDOWN.wait_free(P, "sync")
                rDOWN.wait_free(P, "gpsimd")
                rDOWN.wait_free(P, "vector")
                for C2 in range(22):
                    P.op("sync", lambda E, C2=C2, k=k: E.dma_start(
                        out=dby[:, 256 * C2:256 * C2 + 256],
                        in_=dn_b[128 * C2:128 * C2 + 128,
                                 256 * k:256 * k + 256]), [(LDS, 16)])
                src = dn_s.rearrange("p (g o) -> p g o", o=D)[
                    :, :, 256 * k:256 * k + 256]
                P.op("gpsimd", lambda E, src=src: E.dma_start(
                    out=dsc3, in_=src), [(LDG, 16)])
                P.wait("vector", LDS)
                P.wait("vector", LDG)
                bi = dby.bitcast(I32)
                P.op("vector", lambda E, bi=bi: E.tensor_scalar(
                    dei[:], bi[:], 4, SIGNM,
                    ALU.logical_shift_left, ALU.bitwise_and), [])
                P.op("vector", lambda E, bi=bi: E.tensor_scalar(
                    doi[:], bi[:], 1, M0E,
                    ALU.logical_shift_left, ALU.bitwise_and), [])
                P.op("vector", lambda E: E.tensor_tensor(
                    dei[:], dei[:], doi[:], ALU.bitwise_or), [])
                P.op("vector", lambda E, bi=bi: E.tensor_scalar(
                    doi[:], bi[:], 3, M0E,
                    ALU.logical_shift_right, ALU.bitwise_and), [])
                P.op("vector", lambda E, bi=bi: E.tensor_scalar(
                    bi[:], bi[:], SIGNM, None, ALU.bitwise_and), [])
                P.op("vector", lambda E, bi=bi: E.tensor_tensor(
                    doi[:], doi[:], bi[:], ALU.bitwise_or), [(DV, 1)])
                P.op("vector", lambda E, e8=dei.bitcast(FP8E5):
                     E.tensor_tensor(wde[:], e8[:], dsc[:], ALU.mult),
                     [(DV, 1)])
                P.op("vector", lambda E, o8=doi.bitcast(FP8E5):
                     E.tensor_tensor(wdo[:], o8[:], dsc[:], ALU.mult),
                     [(DV, 1)])
                if KFENCE & 4:
                    fence_dv()
                P.wait("tensor", DV)
                for dtl in range(2):
                    dt = 2 * k + dtl
                    for qi, (l0, w) in enumerate(qch_l):
                        pso = qi & 1
                        psr[pso].wait_free(P, "tensor")
                        for C2 in range(22):
                            P.op("tensor", lambda E, C2=C2, dtl=dtl, l0=l0,
                                 w=w, pso=pso: E.matmul(
                                     ps[pso][:, 0:w],
                                     wde[:, 256 * C2 + 128 * dtl:
                                         256 * C2 + 128 * dtl + 128],
                                     hs(2 * C2, l0, w),
                                     start=(C2 == 0), stop=False,
                                     skip_group_check=True), [])
                            P.op("tensor", lambda E, C2=C2, dtl=dtl, l0=l0,
                                 w=w, pso=pso: E.matmul(
                                     ps[pso][:, 0:w],
                                     wdo[:, 256 * C2 + 128 * dtl:
                                         256 * C2 + 128 * dtl + 128],
                                     hs(2 * C2 + 1, l0, w),
                                     start=False, stop=(C2 == 21),
                                     skip_group_check=True),
                                 [(PE, 1)] if C2 == 21 else [])
                        yt = dt & 1
                        rYT[yt].wait_free(P, "scalar")
                        rYSQ.wait_free(P, "scalar")
                        P.wait("scalar", PE)
                        P.op("scalar", lambda E, w=w, yt=yt, pso=pso:
                             E.activation(ytl[yt][:, 0:w], ps[pso][:, 0:w],
                                          AF.Copy), [(SC, 1)])
                        P.op("scalar", lambda E, w=w, pso=pso: E.activation(
                            ysq[:, 0:w], ps[pso][:, 0:w], AF.Square),
                            [(SC, 1)])
                        if KFENCE & 4:
                            fence_sc()
                        psr[pso].reader(SC, SC.n)
                        P.wait("tensor", SC)
                        P.op("tensor", lambda E, w=w, qi=qi, dt=dt: E.matmul(
                            ps[5 + qi][0:1, 0:w], onesc_sb[:], ysq[:, 0:w],
                            start=(dt == 0), stop=(dt == NT - 1),
                            skip_group_check=True),
                            [(PE, 1)] if dt == NT - 1 else [])
                        rYSQ.reader(PE, PE.n + 1)
                        P.wait("sync", SC)
                        P.op("sync", lambda E, dt=dt, l0=l0, w=w, yt=yt:
                             E.dma_start(out=ybounce[128 * dt:128 * dt + 128,
                                                     l0:l0 + w],
                                         in_=ytl[yt][:, 0:w]), [(ST, 16)])
                        rYT[yt].reader(ST, ST.n)
                rDOWN.reader(PE, PE.n)

            P.barrier_waits()
            for qi, (l0, w) in enumerate(qch_l if KPH >= 5 else []):
                P.op("vector", lambda E, w=w, qi=qi: E.tensor_scalar(
                    row5[0:1, 0:w], ps[5 + qi][0:1, 0:w], 1.0 / D, None,
                    ALU.mult), [(DV, 1)])
                psr[5 + qi].reader(DV, DV.n)
                P.wait("scalar", DV)
                P.op("scalar", lambda E, w=w: E.activation(
                    row5[0:1, 0:w], row5[0:1, 0:w], AF.Ln,
                    bias=eps_sb[0:1, 1:2]), [(SC, 1)])
                P.op("scalar", lambda E, w=w: E.activation(
                    row5[0:1, 0:w], row5[0:1, 0:w], AF.Exp, scale=-0.5),
                    [(SC, 1)])
                if KFENCE & 8:
                    fence_sc()
                psr[2].wait_free(P, "tensor")
                P.wait("tensor", SC)
                P.op("tensor", lambda E, w=w: E.matmul(
                    ps[2][:, 0:w], onesr_sb[:], row5[0:1, 0:w],
                    start=True, stop=True, skip_group_check=True), [(PE, 1)])
                psr[0].wait_free(P, "tensor")
                psr[1].wait_free(P, "tensor")
                P.wait("vector", PE)
                for dt in range(NT if KP5 >= 2 else 0):
                    yb = dt & 1
                    rYB[yb].wait_free(P, "sync")
                    P.op("sync", lambda E, dt=dt, l0=l0, w=w, yb=yb:
                         E.dma_start(out=ybk[yb][:, 0:w],
                                     in_=ybounce[128 * dt:128 * dt + 128,
                                                 l0:l0 + w]), [(LDS, 16)])
                    P.wait("vector", LDS)
                    P.op("vector", lambda E, dt=dt, w=w, yb=yb:
                         E.tensor_tensor(x3v[:, 512 * dt:512 * dt + w],
                                         ybk[yb][:, 0:w], ps[2][:, 0:w],
                                         ALU.mult), [(DV, 1)])
                    P.op("vector", lambda E, dt=dt, l0=l0, w=w:
                         E.tensor_tensor(x3v[:, 512 * dt:512 * dt + w],
                                         x3v[:, 512 * dt:512 * dt + w],
                                         xn(dt, l0, w), ALU.add), [(DV, 1)])
                    rYB[yb].reader(DV, DV.n)
                    if KP5 < 3:
                        continue
                    SUB = os.environ.get("KS3", "abcd")
                    if "a" in SUB:
                        rX3BF.wait_free(P, "vector")
                        P.op("vector", lambda E, dt=dt, w=w: E.tensor_copy(
                            x3bf[:, 0:w], x3v[:, 512 * dt:512 * dt + w]),
                            [(DV, 1)])
                        if KFENCE & 8:
                            fence_dv()
                    if "b" in SUB:
                        rX3SQ.wait_free(P, "scalar")
                        P.wait("scalar", DV)
                        P.op("scalar", lambda E, dt=dt, w=w: E.activation(
                            x3sq[:, 0:w], x3v[:, 512 * dt:512 * dt + w],
                            AF.Square), [(SC, 1)])
                        if KFENCE & 8:
                            fence_sc()
                    P.wait("tensor", DV)
                    P.wait("tensor", SC)
                    if "c" in SUB:
                        P.op("tensor", lambda E, dt=dt, w=w: E.matmul(
                            ps[0][0:1, 0:w], onesc_sb[:], x3bf[:, 0:w],
                            start=(dt == 0), stop=(dt == NT - 1),
                            skip_group_check=True), [(PE, 1)])
                        rX3BF.reader(PE, PE.n)
                    if "d" in SUB:
                        P.op("tensor", lambda E, dt=dt, w=w: E.matmul(
                            ps[1][0:1, 0:w], onesc_sb[:], x3sq[:, 0:w],
                            start=(dt == 0), stop=(dt == NT - 1),
                            skip_group_check=True), [(PE, 1)])
                        rX3SQ.reader(PE, PE.n)
                psr[2].reader(DV, DV.n)
                if KP5 < 4:
                    continue
                P.wait("vector", PE)
                P.op("vector", lambda E, w=w: E.tensor_scalar(
                    row5[0:1, 0:w], ps[0][0:1, 0:w], 1.0 / D, None, ALU.mult),
                    [(DV, 1)])
                P.op("vector", lambda E, w=w: E.tensor_scalar(
                    row5b[0:1, 0:w], ps[1][0:1, 0:w], 1.0 / D, None,
                    ALU.mult), [(DV, 1)])
                psr[0].reader(DV, DV.n)
                psr[1].reader(DV, DV.n)
                P.op("vector", lambda E, w=w: E.tensor_tensor(
                    tmp5[0:1, 0:w], row5[0:1, 0:w], row5[0:1, 0:w],
                    ALU.mult), [(DV, 1)])
                P.op("vector", lambda E, w=w: E.tensor_tensor(
                    row5b[0:1, 0:w], row5b[0:1, 0:w], tmp5[0:1, 0:w],
                    ALU.subtract), [(DV, 1)])
                if KFENCE & 8:
                    fence_dv()
                P.wait("scalar", DV)
                P.op("scalar", lambda E, w=w: E.activation(
                    row5b[0:1, 0:w], row5b[0:1, 0:w], AF.Ln,
                    bias=eps_sb[0:1, 0:1]), [(SC, 1)])
                P.op("scalar", lambda E, w=w: E.activation(
                    row5b[0:1, 0:w], row5b[0:1, 0:w], AF.Exp, scale=-0.5),
                    [(SC, 1)])
                if KFENCE & 8:
                    fence_sc()
                psr[3].wait_free(P, "tensor")
                psr[4].wait_free(P, "tensor")
                P.wait("tensor", DV)
                P.wait("tensor", SC)
                P.op("tensor", lambda E, w=w: E.matmul(
                    ps[3][:, 0:w], onesr_sb[:], row5[0:1, 0:w],
                    start=True, stop=True, skip_group_check=True), [(PE, 1)])
                P.op("tensor", lambda E, w=w: E.matmul(
                    ps[4][:, 0:w], onesr_sb[:], row5b[0:1, 0:w],
                    start=True, stop=True, skip_group_check=True), [(PE, 1)])
                P.wait("vector", PE)
                for dt in range(NT if KP5 >= 5 else 0):
                    ot = dt & 1
                    rOT[ot].wait_free(P, "vector")
                    P.op("vector", lambda E, dt=dt, w=w: E.tensor_tensor(
                        tmp5[:, 0:w], x3v[:, 512 * dt:512 * dt + w],
                        ps[3][:, 0:w], ALU.subtract), [(DV, 1)])
                    P.op("vector", lambda E, w=w, ot=ot: E.tensor_tensor(
                        otl[ot][:, 0:w], tmp5[:, 0:w], ps[4][:, 0:w],
                        ALU.mult), [(DV, 1)])
                    P.wait("sync", DV)
                    P.op("sync", lambda E, dt=dt, l0=l0, w=w, ot=ot:
                         E.dma_start(out=outp[128 * dt:128 * dt + 128,
                                              l0:l0 + w],
                                     in_=otl[ot][:, 0:w]), [(ST, 16)])
                    rOT[ot].reader(ST, ST.n)
                psr[3].reader(DV, DV.n)
                psr[4].reader(DV, DV.n)
                if KSAFE & 4:
                    P.barrier_waits()

            P.wait("gpsimd", ST)
            return P

        plans = [make_plan(0), make_plan(1)]

        def emit(engname):
            def body(E):
                if engname == "sync":
                    E.dma_start(out=role_sb[:], in_=role_d).then_inc(LDS_h, 16)
                    E.dma_start(out=carve(OC, 896, BF16)[:],
                                in_=mask_c.ap()).then_inc(LDS_h, 16)
                    E.dma_start(out=onesc_sb[:],
                                in_=onesc_c.ap()).then_inc(LDS_h, 16)
                    E.dma_start(out=onesr_sb[:],
                                in_=onesr_c.ap()).then_inc(LDS_h, 16)
                E.wait_ge(LDS_h, N_PRE)
                if engname == "vector":
                    E.memset(eps_sb[0:1, 0:1], 1e-5)
                    E.memset(eps_sb[0:1, 1:2],
                             float(1e-6 * 2.0 ** -56)).then_inc(DV_h, 1)
                rv = E.value_load(role_sb[0:1, 0:1])
                for r in E.Switch(rv, 2):
                    for fn in plans[r].streams[engname]:
                        fn(E)
            return body

        block.sync(emit("sync"))
        block.vector(emit("vector"))
        block.scalar(emit("scalar"))
        block.tensor(emit("tensor"))
        block.gpsimd(emit("gpsimd"))

    return nc


_NC_CACHE = None


def _get_nc():
    global _NC_CACHE
    if _NC_CACHE is None:
        _NC_CACHE = build_program()
    return _NC_CACHE


def kernel(L0_attn_x, L0_attn_W_K, L0_attn_W_Q, L0_attn_W_V, L0_attn_W_O,
           L0_ffn_gate_up_packed, L0_ffn_gate_up_scales, L0_ffn_down_packed,
           L0_ffn_down_scales):
    x = np.asarray(L0_attn_x)
    WK = np.asarray(L0_attn_W_K)
    WQ = np.asarray(L0_attn_W_Q)
    WV = np.asarray(L0_attn_W_V)
    WO = np.asarray(L0_attn_W_O)
    gup = np.asarray(L0_ffn_gate_up_packed)
    gus = np.asarray(L0_ffn_gate_up_scales)
    dnp = np.asarray(L0_ffn_down_packed)
    dns = np.asarray(L0_ffn_down_scales)

    wq_t = np.ascontiguousarray(WQ.transpose(2, 0, 1).reshape(D, D)[PI_D])
    wk_t = np.ascontiguousarray(WK.transpose(2, 0, 1).reshape(D, D)[PI_D])
    wv_t = np.ascontiguousarray(WV.transpose(2, 0, 1).reshape(D, D)[PI_D])
    wo_t = np.ascontiguousarray(WO[PI_D].T)

    order = np.empty(2 * II, np.int64)
    for k in range(44):
        order[256 * k:256 * k + 128] = PI_I[128 * k:128 * k + 128]
        order[256 * k + 128:256 * k + 256] = II + PI_I[128 * k:128 * k + 128]
    gu8 = gup.view(np.uint8).reshape(2 * II, D // 2)
    gu_bytes = np.ascontiguousarray(gu8[order].T)
    gusp = gus[:, order]
    gu_scales = np.ascontiguousarray(np.concatenate(
        [np.repeat(gusp[2 * G:2 * G + 2], 64, axis=0) for G in range(8)],
        axis=1))
    dn8 = dnp.view(np.uint8).reshape(D, II // 2)
    dn_bytes = np.ascontiguousarray(dn8[PI_D].T)
    dnsp = dns[:, PI_D]
    dn_scales = np.ascontiguousarray(np.concatenate(
        [np.repeat(dnsp[2 * C:2 * C + 2], 64, axis=0) for C in range(22)],
        axis=1))

    nc = _get_nc()
    in_maps = []
    for c in range(8):
        b, role = c // 2, c % 2
        xTb = np.ascontiguousarray(x[b].T[PI_D])
        in_maps.append(dict(
            role_in=np.array([[role]], np.int32), xT=xTb,
            wq_t=wq_t, wk_t=wk_t, wv_t=wv_t, wo_t=wo_t,
            gu_b=gu_bytes, gu_s=gu_scales, dn_b=dn_bytes, dn_s=dn_scales,
        ))
    res = run_bass_kernel_spmd(nc, in_maps, core_ids=list(range(8)))

    inv = np.argsort(PI_D)
    out = np.empty((4, SEQ, D), np.float32)
    for c in range(8):
        b, role = c // 2, c % 2
        q0, qown = ROLE_CFG[role]["q0"], ROLE_CFG[role]["qown"]
        shard = np.asarray(res.results[c]["outp"])[:, :qown]
        out[b, q0:q0 + qown, :] = shard[inv].T
    return out


# version 08
# speedup vs baseline: 1.0770x; 1.0770x over previous
import os
import numpy as np
import ml_dtypes
from contextlib import ExitStack

KPH = int(os.environ.get("KPH", "9"))
KP5 = int(os.environ.get("KP5", "9"))
KSAFE = int(os.environ.get("KSAFE", "0"))
KFENCE = int(os.environ.get("KFENCE", "15"))

import concourse.bass as bass
import concourse.mybir as mybir
from concourse.bass_utils import run_bass_kernel_spmd

F32 = mybir.dt.float32
BF16 = mybir.dt.bfloat16
U8 = mybir.dt.uint8
I32 = mybir.dt.int32
FP16 = mybir.dt.float16
FP8E5 = mybir.dt.float8e5
AF = mybir.ActivationFunctionType
ALU = mybir.AluOpType
SZ = {F32: 4, BF16: 2, U8: 1, I32: 4, FP8E5: 1, FP16: 2}

D = 2048
SEQ = 2048
NH = 16
NT = D // 128
NIT = 44
II = 5632
INVSQ = float(1.0 / np.sqrt(128.0))
SIGNM = -2139062144
M0E = 0x0E0E0E0E

ROLE_CFG = [
    dict(q0=0, qown=1152, cover=1152, qch=[(0, 512), (512, 512), (1024, 128)]),
    dict(q0=1152, qown=896, cover=2048, qch=[(1152, 512), (1664, 384)]),
]

ARENA_B = 210944
OC = 0
OX = 2048
OZ = 67584
OP = 104448
P1 = 65536


def halfperm(n):
    nb = n // 256
    pi = np.empty(n, np.int64)
    ar = 2 * np.arange(128)
    for c in range(nb):
        pi[256 * c:256 * c + 128] = 256 * c + ar
        pi[256 * c + 128:256 * c + 256] = 256 * c + ar + 1
    return pi


PI_D = halfperm(D)
PI_I = halfperm(II)

ENGS = ("sync", "vector", "scalar", "tensor", "gpsimd")


class Sem:
    def __init__(self, handle, name, init=0):
        self.h = handle
        self.name = name
        self.n = init


class Plan:
    def __init__(self, sems):
        self.streams = {e: [] for e in ENGS}
        self.sems = sems
        self.last_wait = {}

    def wait(self, eng, sem, val=None):
        v = sem.n if val is None else val
        if v <= 0:
            return
        key = (eng, sem.name)
        if self.last_wait.get(key, -1) >= v:
            return
        self.last_wait[key] = v
        self.streams[eng].append(lambda E, h=sem.h, v=v: E.wait_ge(h, v))

    def op(self, eng, fn, incs=()):
        caps = []
        for s, k in incs:
            s.n += k
            caps.append((s.h, k))

        def f(E, fn=fn, caps=caps):
            inst = fn(E)
            for h, k in caps:
                inst = inst.then_inc(h, k)
        self.streams[eng].append(f)

    def barrier_waits(self, engs=None):
        for e in (engs or ENGS):
            for s in self.sems:
                self.wait(e, s)


class Res:
    def __init__(self):
        self.free_at = {}

    def reader(self, sem, val):
        self.free_at[sem] = max(self.free_at.get(sem, 0), val)

    def wait_free(self, plan, eng):
        for s, v in self.free_at.items():
            plan.wait(eng, s, v)


def build_program():
    nc = bass.Bass("TRN2", target_bir_lowering=False)

    role_d = nc.dram_tensor("role_in", [1, 1], I32, kind="ExternalInput").ap()
    xT = nc.dram_tensor("xT", [D, SEQ], F32, kind="ExternalInput").ap()
    wq_t = nc.dram_tensor("wq_t", [D, D], F32, kind="ExternalInput").ap()
    wk_t = nc.dram_tensor("wk_t", [D, D], F32, kind="ExternalInput").ap()
    wv_t = nc.dram_tensor("wv_t", [D, D], F32, kind="ExternalInput").ap()
    wo_t = nc.dram_tensor("wo_t", [D, D], F32, kind="ExternalInput").ap()
    gu_b = nc.dram_tensor("gu_b", [D // 2, 2 * II], U8, kind="ExternalInput").ap()
    gu_s = nc.dram_tensor("gu_s", [128, 8 * 2 * II], FP16,
                          kind="ExternalInput").ap()
    dn_b = nc.dram_tensor("dn_b", [II // 2, D], U8, kind="ExternalInput").ap()
    dn_s = nc.dram_tensor("dn_s", [128, 22 * D], FP16,
                          kind="ExternalInput").ap()
    outp = nc.dram_tensor("outp", [D, 1152], F32, kind="ExternalOutput").ap()
    ybounce = nc.dram_tensor("ybounce", [D, 1152], F32, kind="Internal").ap()

    mask_np = (np.arange(896)[None, :] >= (np.arange(128)[:, None] + 384))
    mask_c = nc.inline_tensor(np.ascontiguousarray(
        mask_np.astype(ml_dtypes.bfloat16)), name="mask_c")
    onesc_c = nc.inline_tensor(np.ones((128, 1), ml_dtypes.bfloat16),
                               name="onesc_c")
    onesr_c = nc.inline_tensor(np.ones((1, 128), np.float32), name="onesr_c")

    es = ExitStack()
    with es:
        arena_t = es.enter_context(nc.sbuf_tensor("arena", [128, ARENA_B], U8))
        role_sb = es.enter_context(nc.sbuf_tensor("role_sb", [1, 1], I32))
        onesc_sb = es.enter_context(nc.sbuf_tensor("onesc", [128, 1], BF16))
        onesr_sb = es.enter_context(nc.sbuf_tensor("onesr", [1, 128], F32))
        eps_sb = es.enter_context(nc.sbuf_tensor("eps", [1, 2], F32))
        ps = [es.enter_context(nc.psum_tensor(f"ps{i}", [128, 512], F32))
              for i in range(8)]
        LDS_h = es.enter_context(nc.semaphore("LDS"))
        LDG_h = es.enter_context(nc.semaphore("LDG"))
        ST_h = es.enter_context(nc.semaphore("ST"))
        PE_h = es.enter_context(nc.semaphore("PE"))
        DV_h = es.enter_context(nc.semaphore("DV"))
        SC_h = es.enter_context(nc.semaphore("SC"))
        block = es.enter_context(nc.Block())

        arena = arena_t[:]

        def carve(off, nel, dt):
            return arena[:, off:off + nel * SZ[dt]].bitcast(dt)

        N_PRE = 4 * 16

        def make_plan(role):
            cfg = ROLE_CFG[role]
            q0, qown, cover = cfg["q0"], cfg["qown"], cfg["cover"]
            qch = cfg["qch"]
            qch_l = [(s - q0, w) for s, w in qch]
            npt = cover // 128

            LDS = Sem(LDS_h, "LDS", N_PRE)
            LDG = Sem(LDG_h, "LDG", 0)
            ST = Sem(ST_h, "ST", 0)
            PE = Sem(PE_h, "PE", 0)
            DV = Sem(DV_h, "DV", 1)
            SC = Sem(SC_h, "SC", 0)
            P = Plan([LDS, LDG, ST, PE, DV, SC])
            psr = [Res() for _ in range(8)]

            mask_sb = carve(OC, 896, BF16)
            xv = carve(OX, NT * SEQ, BF16)

            def xs(t, c0, w):
                return xv[:, t * SEQ + c0: t * SEQ + c0 + w]

            xnc = carve(OX, NT * 1152, BF16)

            def xn(t, l0, w):
                return xnc[:, t * 1152 + l0: t * 1152 + l0 + w]

            zv = carve(OZ, NH * 1152, BF16)

            def zsl(h, l0, w):
                return zv[:, h * 1152 + l0: h * 1152 + l0 + w]

            b1 = OP + P1
            wqv = carve(b1 + 0, NT * 128, BF16)
            wkv = carve(b1 + 4096, NT * 128, BF16)
            wvv = carve(b1 + 8192, NT * 128, BF16)
            qv = carve(b1 + 12288, 1152, BF16)
            kv = carve(b1 + 14592, 2048, BF16)
            vv = carve(b1 + 18688, 2048, BF16)
            atv = [carve(b1 + 22784 + 1024 * i, 512, BF16) for i in range(4)]
            bc1 = carve(b1 + 26880, 512, F32)
            row1 = carve(b1 + 28928, 512, F32)

            wov = carve(OP + 0, NT * D, BF16)
            y1v = carve(OP + 65536, NT * 512, F32)
            y1bf = carve(OP + 98304, 512, BF16)
            sqbf = carve(OP + 99328, 512, BF16)
            tmp2 = carve(OP + 100352, 512, F32)
            row2 = carve(OP + 102400, 512, F32)
            row2b = carve(OP + 104448, 512, F32)

            XSS = OX + 36864
            hbuf = carve(OP + 0, NIT * 1152, BF16)

            def hs(k, l0, w):
                return hbuf[:, k * 1152 + l0: k * 1152 + l0 + w]

            wb = [carve(OZ + 2048 * i, 1024, BF16) for i in range(16)]
            gby = [carve(OZ + 32768 + 1024 * i, 1024, U8) for i in range(2)]
            geo = [carve(OZ + 34816 + 1024 * i, 256, I32) for i in range(2)]
            sexp = carve(XSS + 0, 8192, BF16)
            silu_sb = [carve(XSS + 16384 + 2048 * i, 512, F32)
                       for i in range(2)]
            goo = [carve(XSS + 20480 + 1024 * i, 256, I32) for i in range(2)]
            gbt = carve(XSS + 22528, 256, I32)
            stage = carve(OP + 65536, 1152, BF16)

            wde = carve(OZ + 0, 5632, BF16)
            wdo = carve(OZ + 11264, 5632, BF16)
            dsc = carve(OZ + 22528, 5632, BF16)
            dby = carve(XSS + 0, 5632, U8)
            dei = carve(XSS + 5632, 1408, I32)
            doi = carve(XSS + 11264, 1408, I32)
            ytl = [carve(XSS + 16896 + 2048 * i, 512, F32) for i in range(2)]
            ysq = carve(XSS + 20992, 512, BF16)

            x3v = carve(OP + 0, NT * 512, F32)
            ybk = [carve(OP + 32768 + 2048 * i, 512, F32) for i in range(2)]
            otl = [carve(OP + 36864 + 2048 * i, 512, F32) for i in range(2)]
            x3bf = carve(OP + 40960, 512, BF16)
            x3sq = carve(OP + 41984, 512, BF16)
            tmp5 = carve(OP + 43008, 512, F32)
            row5 = carve(OP + 45056, 512, F32)
            row5b = carve(OP + 47104, 512, F32)

            rW = [Res() for _ in range(3)]
            rAT = [Res() for _ in range(4)]
            rWB = Res()
            rGBY = [Res() for _ in range(2)]
            rSEXP = Res()
            rDOWN = Res()
            rYT = [Res() for _ in range(2)]
            rYB = [Res() for _ in range(2)]
            rOT = [Res() for _ in range(2)]
            rSIL = [Res() for _ in range(2)]
            fdv = carve(OC + 1856, 16, F32)
            fsc = carve(OC + 1920, 16, F32)

            def fence_dv():
                P.op("vector", lambda E: E.tensor_copy(
                    fdv[0:1, 8:16], fdv[0:1, 0:8]), [(DV, 1)])

            def fence_sc():
                P.op("scalar", lambda E: E.copy(
                    fsc[0:1, 8:16], fsc[0:1, 0:8]), [(SC, 1)])

            rY1BF = Res()
            rSQBF = Res()
            rYSQ = Res()
            rX3BF = Res()
            rX3SQ = Res()

            P.op("vector", lambda E: E.memset(fdv[0:1, 0:8], 0.0), [(DV, 1)])
            P.op("scalar", lambda E: E.memzero(fsc[0:1, 0:8]), [(SC, 1)])

            for g4 in range(4):
                xdst = xv.rearrange("p (t c) -> p t c", c=SEQ)[
                    :, 4 * g4:4 * g4 + 4, 0:cover]
                xsrc = xT.rearrange("(t p) c -> p t c", p=128)[
                    :, 4 * g4:4 * g4 + 4, 0:cover]
                P.op("gpsimd", lambda E, o=xdst, i=xsrc:
                     E.dma_start(out=o, in_=i), [(LDG, 16)])

            for g4 in range(4 if KPH >= 2 else 0):
                wod = wov.rearrange("p (t f) -> p t f", f=D)[
                    :, 4 * g4:4 * g4 + 4, :]
                wos = wo_t.rearrange("(t p) f -> p t f", p=128)[
                    :, 4 * g4:4 * g4 + 4, :]
                P.op("gpsimd", lambda E, o=wod, i=wos:
                     E.dma_start(out=o, in_=i), [(LDG, 16)])
            wo_ready = LDG.n
            for h in range(NH if KPH >= 1 else 0):
                for wi, (sbv, dr) in enumerate(
                        [(wqv, wq_t), (wkv, wk_t), (wvv, wv_t)]):
                    rW[wi].wait_free(P, "gpsimd")
                    o3 = sbv.rearrange("p (t f) -> p t f", f=128)
                    i3 = dr.rearrange("(t p) f -> p t f", p=128)[
                        :, :, h * 128:(h + 1) * 128]
                    P.op("gpsimd", lambda E, o=o3, i=i3:
                         E.dma_start(out=o, in_=i), [(LDG, 16)])
                P.wait("tensor", LDG)

                wq3 = wqv.rearrange("p (t f) -> p t f", f=128)
                wk3 = wkv.rearrange("p (t f) -> p t f", f=128)
                wv3 = wvv.rearrange("p (t f) -> p t f", f=128)

                proj_jobs = [("q", l0, w) for (l0, w) in qch_l]
                c0 = 0
                while c0 < cover:
                    w = min(512, cover - c0)
                    proj_jobs.append(("k", c0, w))
                    c0 += w
                proj_jobs += [("v", pt * 128, 128) for pt in range(npt)]
                pjc = 0
                for kind, c0, w in proj_jobs:
                    pso = pjc & 1
                    pjc += 1
                    psr[pso].wait_free(P, "tensor")
                    for t in range(NT):
                        if kind == "v":
                            P.op("tensor", lambda E, t=t, c0=c0, pso=pso:
                                 E.matmul(ps[pso][:, 0:128],
                                          xs(t, c0, 128), wv3[:, t, :],
                                          start=(t == 0), stop=(t == NT - 1),
                                          skip_group_check=True),
                                 [(PE, 1)] if t == NT - 1 else [])
                        else:
                            wmat = wq3 if kind == "q" else wk3
                            xc0 = (q0 + c0) if kind == "q" else c0
                            P.op("tensor", lambda E, t=t, xc0=xc0, w=w,
                                 wmat=wmat, pso=pso: E.matmul(
                                     ps[pso][:, 0:w], wmat[:, t, :],
                                     xs(t, xc0, w),
                                     start=(t == 0), stop=(t == NT - 1),
                                     skip_group_check=True),
                                 [(PE, 1)] if t == NT - 1 else [])
                    if kind == "q":
                        q_done = PE.n
                    elif kind == "k":
                        k_done = PE.n
                    else:
                        v_done = PE.n
                    P.wait("scalar", PE)
                    dst = {"q": qv, "k": kv, "v": vv}[kind]
                    P.op("scalar", lambda E, c0=c0, w=w, dst=dst, pso=pso:
                         E.activation(dst[:, c0:c0 + w], ps[pso][:, 0:w],
                                      AF.Copy), [(SC, 1)])
                    psr[pso].reader(SC, SC.n)
                rW[0].reader(PE, q_done)
                rW[1].reader(PE, k_done)
                rW[2].reader(PE, v_done)
                fence_sc()

                for (qg0, w), (l0, _w2) in zip(qch, qch_l):
                    pmax = (qg0 + w - 1) // 128
                    P.wait("tensor", SC)
                    sc_after = {}
                    dv_after = {}

                    def emit_s_exp(pt):
                        pss = 1 + (pt & 1)
                        slot = pt % 4
                        psr[pss].wait_free(P, "tensor")
                        P.op("tensor", lambda E, pt=pt, pss=pss, l0=l0, w=w:
                             E.matmul(ps[pss][:, 0:w],
                                      kv[:, pt * 128:pt * 128 + 128],
                                      qv[:, l0:l0 + w], start=True, stop=True,
                                      skip_group_check=True), [(PE, 1)])
                        rAT[slot].wait_free(P, "scalar")
                        P.wait("scalar", PE)
                        P.op("scalar", lambda E, pss=pss, slot=slot, w=w:
                             E.activation(atv[slot][:, 0:w], ps[pss][:, 0:w],
                                          AF.Exp, scale=INVSQ), [(SC, 1)])
                        sc_after[pt] = SC.n
                        psr[pss].reader(SC, SC.n)
                        delta = qg0 - 128 * pt
                        if delta < 128:
                            fence_sc()
                            P.wait("vector", SC)
                            P.op("vector", lambda E, slot=slot, w=w, d=delta:
                                 E.tensor_tensor(
                                     atv[slot][:, 0:w], atv[slot][:, 0:w],
                                     mask_sb[:, 384 + d:384 + d + w],
                                     ALU.mult), [(DV, 1)])
                            fence_dv()
                            dv_after[pt] = DV.n

                    def emit_zsig(pt):
                        slot = pt % 4
                        if pt in dv_after:
                            P.wait("tensor", DV, dv_after[pt])
                        wait_sc = sc_after.get(pt + 1)
                        if wait_sc is None:
                            fence_sc()
                            wait_sc = SC.n
                        P.wait("tensor", SC, wait_sc)
                        if pt == 0:
                            psr[3].wait_free(P, "tensor")
                            psr[4].wait_free(P, "tensor")
                        P.op("tensor", lambda E, pt=pt, slot=slot, w=w,
                             pmax=pmax: E.matmul(
                                 ps[3][:, 0:w],
                                 vv[:, pt * 128:pt * 128 + 128],
                                 atv[slot][:, 0:w],
                                 start=(pt == 0), stop=(pt == pmax),
                                 skip_group_check=True),
                             [(PE, 1)] if pt == pmax else [])
                        P.op("tensor", lambda E, pt=pt, slot=slot, w=w,
                             pmax=pmax: E.matmul(
                                 ps[4][0:1, 0:w], onesc_sb[:],
                                 atv[slot][:, 0:w],
                                 start=(pt == 0), stop=(pt == pmax),
                                 skip_group_check=True),
                             [(PE, 1)] if pt == pmax else [])
                        rAT[slot].reader(PE, PE.n + 1)

                    emit_s_exp(0)
                    for pt in range(1, pmax + 1):
                        emit_s_exp(pt)
                        emit_zsig(pt - 1)
                    emit_zsig(pmax)
                    P.wait("vector", PE)
                    P.op("vector", lambda E, w=w: E.tensor_copy(
                        bc1[0:1, 0:w], ps[4][0:1, 0:w]), [(DV, 1)])
                    psr[4].reader(DV, DV.n)
                    P.op("vector", lambda E, w=w: E.reciprocal(
                        row1[0:1, 0:w], bc1[0:1, 0:w]), [(DV, 1)])
                    if KFENCE & 1:
                        fence_dv()
                    psr[5].wait_free(P, "tensor")
                    P.wait("tensor", DV)
                    P.op("tensor", lambda E, w=w: E.matmul(
                        ps[5][:, 0:w], onesr_sb[:], row1[0:1, 0:w],
                        start=True, stop=True, skip_group_check=True),
                        [(PE, 1)])
                    P.wait("vector", PE)
                    P.op("vector", lambda E, w=w: E.tensor_copy(
                        bc1[:, 0:w], ps[5][:, 0:w]), [(DV, 1)])
                    psr[5].reader(DV, DV.n)
                    P.op("vector", lambda E, h=h, l0=l0, w=w: E.tensor_tensor(
                        zsl(h, l0, w), ps[3][:, 0:w], bc1[:, 0:w], ALU.mult),
                        [(DV, 1)])
                    psr[3].reader(DV, DV.n)
                    if KFENCE & 1:
                        fence_dv()
                    if KSAFE & 2:
                        P.barrier_waits()
                for wi in range(3):
                    rW[wi].reader(PE, PE.n)

            P.wait("tensor", LDG, wo_ready)
            P.barrier_waits(("tensor", "vector", "scalar"))
            wo3s = wov.rearrange("p (t f) -> p t f", f=D)

            for (qg0, w), (l0, _w2) in zip(qch if KPH >= 2 else [], qch_l):
                psr[6].wait_free(P, "tensor")
                psr[7].wait_free(P, "tensor")
                for dt in range(NT):
                    pso = dt & 1
                    psr[pso].wait_free(P, "tensor")
                    for ft in range(NH):
                        P.op("tensor", lambda E, dt=dt, ft=ft, l0=l0, w=w,
                             pso=pso: E.matmul(
                                 ps[pso][:, 0:w],
                                 wo3s[:, ft, dt * 128:dt * 128 + 128],
                                 zsl(ft, l0, w),
                                 start=(ft == 0), stop=(ft == NH - 1),
                                 skip_group_check=True),
                             [(PE, 1)] if ft == NH - 1 else [])
                    P.wait("vector", PE)
                    P.op("vector", lambda E, dt=dt, l0=l0, w=w, pso=pso:
                         E.tensor_tensor(y1v[:, dt * 512:dt * 512 + w],
                                         ps[pso][:, 0:w],
                                         xs(dt, q0 + l0, w), ALU.add),
                         [(DV, 1)])
                    psr[pso].reader(DV, DV.n)
                    rY1BF.wait_free(P, "vector")
                    P.op("vector", lambda E, dt=dt, w=w: E.tensor_copy(
                        y1bf[:, 0:w], y1v[:, dt * 512:dt * 512 + w]),
                        [(DV, 1)])
                    if KFENCE & 2:
                        fence_dv()
                    rSQBF.wait_free(P, "scalar")
                    P.wait("scalar", DV)
                    P.op("scalar", lambda E, dt=dt, w=w: E.activation(
                        sqbf[:, 0:w], y1v[:, dt * 512:dt * 512 + w],
                        AF.Square), [(SC, 1)])
                    if KFENCE & 2:
                        fence_sc()
                    P.wait("tensor", DV)
                    P.wait("tensor", SC)
                    P.op("tensor", lambda E, dt=dt, w=w: E.matmul(
                        ps[6][0:1, 0:w], onesc_sb[:], y1bf[:, 0:w],
                        start=(dt == 0), stop=(dt == NT - 1),
                        skip_group_check=True),
                        [(PE, 1)] if dt == NT - 1 else [])
                    P.op("tensor", lambda E, dt=dt, w=w: E.matmul(
                        ps[7][0:1, 0:w], onesc_sb[:], sqbf[:, 0:w],
                        start=(dt == 0), stop=(dt == NT - 1),
                        skip_group_check=True),
                        [(PE, 1)] if dt == NT - 1 else [])
                    rY1BF.reader(PE, PE.n + 1)
                    rSQBF.reader(PE, PE.n + 1)
                P.wait("vector", PE)
                P.op("vector", lambda E, w=w: E.tensor_scalar(
                    row2[0:1, 0:w], ps[6][0:1, 0:w], 1.0 / D, None, ALU.mult),
                    [(DV, 1)])
                P.op("vector", lambda E, w=w: E.tensor_scalar(
                    row2b[0:1, 0:w], ps[7][0:1, 0:w], 1.0 / D, None,
                    ALU.mult), [(DV, 1)])
                psr[6].reader(DV, DV.n)
                psr[7].reader(DV, DV.n)
                P.op("vector", lambda E, w=w: E.tensor_tensor(
                    tmp2[0:1, 0:w], row2[0:1, 0:w], row2[0:1, 0:w],
                    ALU.mult), [(DV, 1)])
                P.op("vector", lambda E, w=w: E.tensor_tensor(
                    row2b[0:1, 0:w], row2b[0:1, 0:w], tmp2[0:1, 0:w],
                    ALU.subtract), [(DV, 1)])
                if KFENCE & 2:
                    fence_dv()
                P.wait("scalar", DV)
                P.op("scalar", lambda E, w=w: E.activation(
                    row2b[0:1, 0:w], row2b[0:1, 0:w], AF.Ln,
                    bias=eps_sb[0:1, 0:1]), [(SC, 1)])
                P.op("scalar", lambda E, w=w: E.activation(
                    row2b[0:1, 0:w], row2b[0:1, 0:w], AF.Exp, scale=-0.5),
                    [(SC, 1)])
                if KFENCE & 2:
                    fence_sc()
                psr[2].wait_free(P, "tensor")
                psr[3].wait_free(P, "tensor")
                P.wait("tensor", DV)
                P.wait("tensor", SC)
                P.op("tensor", lambda E, w=w: E.matmul(
                    ps[2][:, 0:w], onesr_sb[:], row2[0:1, 0:w],
                    start=True, stop=True, skip_group_check=True), [(PE, 1)])
                P.op("tensor", lambda E, w=w: E.matmul(
                    ps[3][:, 0:w], onesr_sb[:], row2b[0:1, 0:w],
                    start=True, stop=True, skip_group_check=True), [(PE, 1)])
                P.wait("vector", PE)
                for dt in range(NT):
                    P.op("vector", lambda E, dt=dt, w=w: E.tensor_tensor(
                        tmp2[:, 0:w], y1v[:, dt * 512:dt * 512 + w],
                        ps[2][:, 0:w], ALU.subtract), [(DV, 1)])
                    P.op("vector", lambda E, dt=dt, l0=l0, w=w:
                         E.tensor_tensor(xs(dt, q0 + l0, w), tmp2[:, 0:w],
                                         ps[3][:, 0:w], ALU.mult), [(DV, 1)])
                psr[2].reader(DV, DV.n)
                psr[3].reader(DV, DV.n)
                if KSAFE & 1:
                    P.barrier_waits()

            for t in range(NT if KPH >= 2 else 0):
                P.op("vector", lambda E, t=t: E.tensor_copy(
                    stage[:, 0:qown], xs(t, q0, qown)), [(DV, 1)])
                P.op("vector", lambda E, t=t: E.tensor_copy(
                    xnc[:, 1152 * t:1152 * t + qown], stage[:, 0:qown]),
                    [(DV, 1)])
            if KFENCE & 4:
                fence_dv()

            P.barrier_waits()
            sexp3 = sexp.rearrange("p (g c) -> p g c", c=1024)
            for c in range(11 if KPH >= 3 else 0):
                rSEXP.wait_free(P, "gpsimd")
                src = gu_s.rearrange("p (g o) -> p g o", o=2 * II)[
                    :, :, 1024 * c:1024 * c + 1024]
                P.op("gpsimd", lambda E, src=src: E.dma_start(
                    out=sexp3, in_=src), [(LDG, 16)])
                sexp_ready = LDG.n
                for G in range(8):
                    gs = G & 1
                    rGBY[gs].wait_free(P, "sync")
                    P.op("sync", lambda E, G=G, c=c, gs=gs: E.dma_start(
                        out=gby[gs][:], in_=gu_b[128 * G:128 * G + 128,
                                                 1024 * c:1024 * c + 1024]),
                        [(LDS, 16)])
                    by_ready = LDS.n
                    bi = gby[gs].bitcast(I32)
                    ei, oi, ti = geo[gs], goo[gs], gbt
                    P.wait("vector", LDS, by_ready)
                    if G == 0:
                        rWB.wait_free(P, "vector")
                    P.op("vector", lambda E, bi=bi, ei=ei: E.tensor_scalar(
                        ei[:], bi[:], 4, SIGNM,
                        ALU.logical_shift_left, ALU.bitwise_and), [])
                    P.op("vector", lambda E, bi=bi, ti=ti: E.tensor_scalar(
                        ti[:], bi[:], 1, M0E,
                        ALU.logical_shift_left, ALU.bitwise_and), [])
                    P.op("vector", lambda E, ei=ei, ti=ti: E.tensor_tensor(
                        ei[:], ei[:], ti[:], ALU.bitwise_or), [])
                    P.op("vector", lambda E, bi=bi, oi=oi: E.tensor_scalar(
                        oi[:], bi[:], 3, M0E,
                        ALU.logical_shift_right, ALU.bitwise_and), [])
                    P.op("vector", lambda E, bi=bi, ti=ti: E.tensor_scalar(
                        ti[:], bi[:], SIGNM, None, ALU.bitwise_and), [])
                    P.op("vector", lambda E, oi=oi, ti=ti: E.tensor_tensor(
                        oi[:], oi[:], ti[:], ALU.bitwise_or), [(DV, 1)])
                    rGBY[gs].reader(DV, DV.n)
                    P.wait("vector", LDG, sexp_ready)
                    P.op("vector", lambda E, G=G, e8=ei.bitcast(FP8E5):
                         E.tensor_tensor(
                             wb[2 * G][:], e8[:],
                             sexp[:, G * 1024:G * 1024 + 1024], ALU.mult),
                         [(DV, 1)])
                    P.op("vector", lambda E, G=G, o8=oi.bitcast(FP8E5):
                         E.tensor_tensor(
                             wb[2 * G + 1][:], o8[:],
                             sexp[:, G * 1024:G * 1024 + 1024], ALU.mult),
                         [(DV, 1)])
                rSEXP.reader(DV, DV.n)
                if KFENCE & 4:
                    fence_dv()
                P.wait("tensor", DV)
                for pr in range(4):
                    kk = 4 * c + pr
                    for (l0, w) in qch_l:
                        psr[0].wait_free(P, "tensor")
                        psr[1].wait_free(P, "tensor")
                        for t in range(NT):
                            P.op("tensor", lambda E, t=t, pr=pr, l0=l0, w=w:
                                 E.matmul(
                                     ps[0][:, 0:w],
                                     wb[t][:, 256 * pr:256 * pr + 128],
                                     xn(t, l0, w),
                                     start=(t == 0), stop=(t == NT - 1),
                                     skip_group_check=True),
                                 [(PE, 1)] if t == NT - 1 else [])
                        for t in range(NT):
                            P.op("tensor", lambda E, t=t, pr=pr, l0=l0, w=w:
                                 E.matmul(
                                     ps[1][:, 0:w],
                                     wb[t][:, 256 * pr + 128:256 * pr + 256],
                                     xn(t, l0, w),
                                     start=(t == 0), stop=(t == NT - 1),
                                     skip_group_check=True),
                                 [(PE, 1)] if t == NT - 1 else [])
                        ssl = pr & 1
                        rSIL[ssl].wait_free(P, "scalar")
                        P.wait("scalar", PE)
                        P.op("scalar", lambda E, w=w, ssl=ssl: E.activation(
                            silu_sb[ssl][:, 0:w], ps[0][:, 0:w], AF.Silu,
                            scale=16384.0), [(SC, 1)])
                        if KFENCE & 4:
                            fence_sc()
                        psr[0].reader(SC, SC.n)
                        P.wait("vector", SC)
                        P.op("vector", lambda E, kk=kk, l0=l0, w=w, ssl=ssl:
                             E.tensor_tensor(hs(kk, l0, w),
                                             silu_sb[ssl][:, 0:w],
                                             ps[1][:, 0:w], ALU.mult),
                             [(DV, 1)])
                        psr[1].reader(DV, DV.n)
                        rSIL[ssl].reader(DV, DV.n)
                rWB.reader(PE, PE.n)

            P.barrier_waits()
            dsc3 = dsc.rearrange("p (g c) -> p g c", c=256)
            nqc = len(qch_l)
            for qi in range(nqc if KPH >= 4 else 0):
                psr[5 + qi].wait_free(P, "tensor")
            for k in range(8 if KPH >= 4 else 0):
                rDOWN.wait_free(P, "sync")
                rDOWN.wait_free(P, "gpsimd")
                rDOWN.wait_free(P, "vector")
                for C2 in range(22):
                    P.op("sync", lambda E, C2=C2, k=k: E.dma_start(
                        out=dby[:, 256 * C2:256 * C2 + 256],
                        in_=dn_b[128 * C2:128 * C2 + 128,
                                 256 * k:256 * k + 256]), [(LDS, 16)])
                src = dn_s.rearrange("p (g o) -> p g o", o=D)[
                    :, :, 256 * k:256 * k + 256]
                P.op("gpsimd", lambda E, src=src: E.dma_start(
                    out=dsc3, in_=src), [(LDG, 16)])
                P.wait("vector", LDS)
                P.wait("vector", LDG)
                bi = dby.bitcast(I32)
                P.op("vector", lambda E, bi=bi: E.tensor_scalar(
                    dei[:], bi[:], 4, SIGNM,
                    ALU.logical_shift_left, ALU.bitwise_and), [])
                P.op("vector", lambda E, bi=bi: E.tensor_scalar(
                    doi[:], bi[:], 1, M0E,
                    ALU.logical_shift_left, ALU.bitwise_and), [])
                P.op("vector", lambda E: E.tensor_tensor(
                    dei[:], dei[:], doi[:], ALU.bitwise_or), [])
                P.op("vector", lambda E, bi=bi: E.tensor_scalar(
                    doi[:], bi[:], 3, M0E,
                    ALU.logical_shift_right, ALU.bitwise_and), [])
                P.op("vector", lambda E, bi=bi: E.tensor_scalar(
                    bi[:], bi[:], SIGNM, None, ALU.bitwise_and), [])
                P.op("vector", lambda E, bi=bi: E.tensor_tensor(
                    doi[:], doi[:], bi[:], ALU.bitwise_or), [(DV, 1)])
                P.op("vector", lambda E, e8=dei.bitcast(FP8E5):
                     E.tensor_tensor(wde[:], e8[:], dsc[:], ALU.mult),
                     [(DV, 1)])
                P.op("vector", lambda E, o8=doi.bitcast(FP8E5):
                     E.tensor_tensor(wdo[:], o8[:], dsc[:], ALU.mult),
                     [(DV, 1)])
                if KFENCE & 4:
                    fence_dv()
                P.wait("tensor", DV)
                for dtl in range(2):
                    dt = 2 * k + dtl
                    for qi, (l0, w) in enumerate(qch_l):
                        pso = qi & 1
                        psr[pso].wait_free(P, "tensor")
                        for C2 in range(22):
                            P.op("tensor", lambda E, C2=C2, dtl=dtl, l0=l0,
                                 w=w, pso=pso: E.matmul(
                                     ps[pso][:, 0:w],
                                     wde[:, 256 * C2 + 128 * dtl:
                                         256 * C2 + 128 * dtl + 128],
                                     hs(2 * C2, l0, w),
                                     start=(C2 == 0), stop=False,
                                     skip_group_check=True), [])
                            P.op("tensor", lambda E, C2=C2, dtl=dtl, l0=l0,
                                 w=w, pso=pso: E.matmul(
                                     ps[pso][:, 0:w],
                                     wdo[:, 256 * C2 + 128 * dtl:
                                         256 * C2 + 128 * dtl + 128],
                                     hs(2 * C2 + 1, l0, w),
                                     start=False, stop=(C2 == 21),
                                     skip_group_check=True),
                                 [(PE, 1)] if C2 == 21 else [])
                        yt = dt & 1
                        rYT[yt].wait_free(P, "scalar")
                        rYSQ.wait_free(P, "scalar")
                        P.wait("scalar", PE)
                        P.op("scalar", lambda E, w=w, yt=yt, pso=pso:
                             E.activation(ytl[yt][:, 0:w], ps[pso][:, 0:w],
                                          AF.Copy), [(SC, 1)])
                        P.op("scalar", lambda E, w=w, pso=pso: E.activation(
                            ysq[:, 0:w], ps[pso][:, 0:w], AF.Square),
                            [(SC, 1)])
                        if KFENCE & 4:
                            fence_sc()
                        psr[pso].reader(SC, SC.n)
                        P.wait("tensor", SC)
                        P.op("tensor", lambda E, w=w, qi=qi, dt=dt: E.matmul(
                            ps[5 + qi][0:1, 0:w], onesc_sb[:], ysq[:, 0:w],
                            start=(dt == 0), stop=(dt == NT - 1),
                            skip_group_check=True),
                            [(PE, 1)] if dt == NT - 1 else [])
                        rYSQ.reader(PE, PE.n + 1)
                        P.wait("sync", SC)
                        P.op("sync", lambda E, dt=dt, l0=l0, w=w, yt=yt:
                             E.dma_start(out=ybounce[128 * dt:128 * dt + 128,
                                                     l0:l0 + w],
                                         in_=ytl[yt][:, 0:w]), [(ST, 16)])
                        rYT[yt].reader(ST, ST.n)
                rDOWN.reader(PE, PE.n)

            P.barrier_waits()
            for qi, (l0, w) in enumerate(qch_l if KPH >= 5 else []):
                P.op("vector", lambda E, w=w, qi=qi: E.tensor_scalar(
                    row5[0:1, 0:w], ps[5 + qi][0:1, 0:w], 1.0 / D, None,
                    ALU.mult), [(DV, 1)])
                psr[5 + qi].reader(DV, DV.n)
                P.wait("scalar", DV)
                P.op("scalar", lambda E, w=w: E.activation(
                    row5[0:1, 0:w], row5[0:1, 0:w], AF.Ln,
                    bias=eps_sb[0:1, 1:2]), [(SC, 1)])
                P.op("scalar", lambda E, w=w: E.activation(
                    row5[0:1, 0:w], row5[0:1, 0:w], AF.Exp, scale=-0.5),
                    [(SC, 1)])
                if KFENCE & 8:
                    fence_sc()
                psr[2].wait_free(P, "tensor")
                P.wait("tensor", SC)
                P.op("tensor", lambda E, w=w: E.matmul(
                    ps[2][:, 0:w], onesr_sb[:], row5[0:1, 0:w],
                    start=True, stop=True, skip_group_check=True), [(PE, 1)])
                psr[0].wait_free(P, "tensor")
                psr[1].wait_free(P, "tensor")
                P.wait("vector", PE)
                for dt in range(NT if KP5 >= 2 else 0):
                    yb = dt & 1
                    rYB[yb].wait_free(P, "sync")
                    P.op("sync", lambda E, dt=dt, l0=l0, w=w, yb=yb:
                         E.dma_start(out=ybk[yb][:, 0:w],
                                     in_=ybounce[128 * dt:128 * dt + 128,
                                                 l0:l0 + w]), [(LDS, 16)])
                    P.wait("vector", LDS)
                    P.op("vector", lambda E, dt=dt, w=w, yb=yb:
                         E.tensor_tensor(x3v[:, 512 * dt:512 * dt + w],
                                         ybk[yb][:, 0:w], ps[2][:, 0:w],
                                         ALU.mult), [(DV, 1)])
                    P.op("vector", lambda E, dt=dt, l0=l0, w=w:
                         E.tensor_tensor(x3v[:, 512 * dt:512 * dt + w],
                                         x3v[:, 512 * dt:512 * dt + w],
                                         xn(dt, l0, w), ALU.add), [(DV, 1)])
                    rYB[yb].reader(DV, DV.n)
                    if KP5 < 3:
                        continue
                    SUB = os.environ.get("KS3", "abcd")
                    if "a" in SUB:
                        rX3BF.wait_free(P, "vector")
                        P.op("vector", lambda E, dt=dt, w=w: E.tensor_copy(
                            x3bf[:, 0:w], x3v[:, 512 * dt:512 * dt + w]),
                            [(DV, 1)])
                        if KFENCE & 8:
                            fence_dv()
                    if "b" in SUB:
                        rX3SQ.wait_free(P, "scalar")
                        P.wait("scalar", DV)
                        P.op("scalar", lambda E, dt=dt, w=w: E.activation(
                            x3sq[:, 0:w], x3v[:, 512 * dt:512 * dt + w],
                            AF.Square), [(SC, 1)])
                        if KFENCE & 8:
                            fence_sc()
                    P.wait("tensor", DV)
                    P.wait("tensor", SC)
                    if "c" in SUB:
                        P.op("tensor", lambda E, dt=dt, w=w: E.matmul(
                            ps[0][0:1, 0:w], onesc_sb[:], x3bf[:, 0:w],
                            start=(dt == 0), stop=(dt == NT - 1),
                            skip_group_check=True), [(PE, 1)])
                        rX3BF.reader(PE, PE.n)
                    if "d" in SUB:
                        P.op("tensor", lambda E, dt=dt, w=w: E.matmul(
                            ps[1][0:1, 0:w], onesc_sb[:], x3sq[:, 0:w],
                            start=(dt == 0), stop=(dt == NT - 1),
                            skip_group_check=True), [(PE, 1)])
                        rX3SQ.reader(PE, PE.n)
                psr[2].reader(DV, DV.n)
                if KP5 < 4:
                    continue
                P.wait("vector", PE)
                P.op("vector", lambda E, w=w: E.tensor_scalar(
                    row5[0:1, 0:w], ps[0][0:1, 0:w], 1.0 / D, None, ALU.mult),
                    [(DV, 1)])
                P.op("vector", lambda E, w=w: E.tensor_scalar(
                    row5b[0:1, 0:w], ps[1][0:1, 0:w], 1.0 / D, None,
                    ALU.mult), [(DV, 1)])
                psr[0].reader(DV, DV.n)
                psr[1].reader(DV, DV.n)
                P.op("vector", lambda E, w=w: E.tensor_tensor(
                    tmp5[0:1, 0:w], row5[0:1, 0:w], row5[0:1, 0:w],
                    ALU.mult), [(DV, 1)])
                P.op("vector", lambda E, w=w: E.tensor_tensor(
                    row5b[0:1, 0:w], row5b[0:1, 0:w], tmp5[0:1, 0:w],
                    ALU.subtract), [(DV, 1)])
                if KFENCE & 8:
                    fence_dv()
                P.wait("scalar", DV)
                P.op("scalar", lambda E, w=w: E.activation(
                    row5b[0:1, 0:w], row5b[0:1, 0:w], AF.Ln,
                    bias=eps_sb[0:1, 0:1]), [(SC, 1)])
                P.op("scalar", lambda E, w=w: E.activation(
                    row5b[0:1, 0:w], row5b[0:1, 0:w], AF.Exp, scale=-0.5),
                    [(SC, 1)])
                if KFENCE & 8:
                    fence_sc()
                psr[3].wait_free(P, "tensor")
                psr[4].wait_free(P, "tensor")
                P.wait("tensor", DV)
                P.wait("tensor", SC)
                P.op("tensor", lambda E, w=w: E.matmul(
                    ps[3][:, 0:w], onesr_sb[:], row5[0:1, 0:w],
                    start=True, stop=True, skip_group_check=True), [(PE, 1)])
                P.op("tensor", lambda E, w=w: E.matmul(
                    ps[4][:, 0:w], onesr_sb[:], row5b[0:1, 0:w],
                    start=True, stop=True, skip_group_check=True), [(PE, 1)])
                P.wait("vector", PE)
                for dt in range(NT if KP5 >= 5 else 0):
                    ot = dt & 1
                    rOT[ot].wait_free(P, "vector")
                    P.op("vector", lambda E, dt=dt, w=w: E.tensor_tensor(
                        tmp5[:, 0:w], x3v[:, 512 * dt:512 * dt + w],
                        ps[3][:, 0:w], ALU.subtract), [(DV, 1)])
                    P.op("vector", lambda E, w=w, ot=ot: E.tensor_tensor(
                        otl[ot][:, 0:w], tmp5[:, 0:w], ps[4][:, 0:w],
                        ALU.mult), [(DV, 1)])
                    P.wait("sync", DV)
                    P.op("sync", lambda E, dt=dt, l0=l0, w=w, ot=ot:
                         E.dma_start(out=outp[128 * dt:128 * dt + 128,
                                              l0:l0 + w],
                                     in_=otl[ot][:, 0:w]), [(ST, 16)])
                    rOT[ot].reader(ST, ST.n)
                psr[3].reader(DV, DV.n)
                psr[4].reader(DV, DV.n)
                if KSAFE & 4:
                    P.barrier_waits()

            P.wait("gpsimd", ST)
            return P

        plans = [make_plan(0), make_plan(1)]

        def emit(engname):
            def body(E):
                if engname == "sync":
                    E.dma_start(out=role_sb[:], in_=role_d).then_inc(LDS_h, 16)
                    E.dma_start(out=carve(OC, 896, BF16)[:],
                                in_=mask_c.ap()).then_inc(LDS_h, 16)
                    E.dma_start(out=onesc_sb[:],
                                in_=onesc_c.ap()).then_inc(LDS_h, 16)
                    E.dma_start(out=onesr_sb[:],
                                in_=onesr_c.ap()).then_inc(LDS_h, 16)
                E.wait_ge(LDS_h, N_PRE)
                if engname == "vector":
                    E.memset(eps_sb[0:1, 0:1], 1e-5)
                    E.memset(eps_sb[0:1, 1:2],
                             float(1e-6 * 2.0 ** -56)).then_inc(DV_h, 1)
                rv = E.value_load(role_sb[0:1, 0:1])
                for r in E.Switch(rv, 2):
                    for fn in plans[r].streams[engname]:
                        fn(E)
            return body

        block.sync(emit("sync"))
        block.vector(emit("vector"))
        block.scalar(emit("scalar"))
        block.tensor(emit("tensor"))
        block.gpsimd(emit("gpsimd"))

    return nc


_NC_CACHE = None


def _get_nc():
    global _NC_CACHE
    if _NC_CACHE is None:
        _NC_CACHE = build_program()
    return _NC_CACHE


def kernel(L0_attn_x, L0_attn_W_K, L0_attn_W_Q, L0_attn_W_V, L0_attn_W_O,
           L0_ffn_gate_up_packed, L0_ffn_gate_up_scales, L0_ffn_down_packed,
           L0_ffn_down_scales):
    x = np.asarray(L0_attn_x)
    WK = np.asarray(L0_attn_W_K)
    WQ = np.asarray(L0_attn_W_Q)
    WV = np.asarray(L0_attn_W_V)
    WO = np.asarray(L0_attn_W_O)
    gup = np.asarray(L0_ffn_gate_up_packed)
    gus = np.asarray(L0_ffn_gate_up_scales)
    dnp = np.asarray(L0_ffn_down_packed)
    dns = np.asarray(L0_ffn_down_scales)

    wq_t = np.ascontiguousarray(WQ.transpose(2, 0, 1).reshape(D, D)[PI_D])
    wk_t = np.ascontiguousarray(WK.transpose(2, 0, 1).reshape(D, D)[PI_D])
    wv_t = np.ascontiguousarray(WV.transpose(2, 0, 1).reshape(D, D)[PI_D])
    wo_t = np.ascontiguousarray(WO[PI_D].T)

    order = np.empty(2 * II, np.int64)
    for k in range(44):
        order[256 * k:256 * k + 128] = PI_I[128 * k:128 * k + 128]
        order[256 * k + 128:256 * k + 256] = II + PI_I[128 * k:128 * k + 128]
    gu8 = gup.view(np.uint8).reshape(2 * II, D // 2)
    gu_bytes = np.ascontiguousarray(gu8[order].T)
    gusp = gus[:, order]
    gu_scales = np.ascontiguousarray(np.concatenate(
        [np.repeat(gusp[2 * G:2 * G + 2], 64, axis=0) for G in range(8)],
        axis=1))
    dn8 = dnp.view(np.uint8).reshape(D, II // 2)
    dn_bytes = np.ascontiguousarray(dn8[PI_D].T)
    dnsp = dns[:, PI_D]
    dn_scales = np.ascontiguousarray(np.concatenate(
        [np.repeat(dnsp[2 * C:2 * C + 2], 64, axis=0) for C in range(22)],
        axis=1))

    nc = _get_nc()
    in_maps = []
    for c in range(8):
        b, role = c // 2, c % 2
        xTb = np.ascontiguousarray(x[b].T[PI_D])
        in_maps.append(dict(
            role_in=np.array([[role]], np.int32), xT=xTb,
            wq_t=wq_t, wk_t=wk_t, wv_t=wv_t, wo_t=wo_t,
            gu_b=gu_bytes, gu_s=gu_scales, dn_b=dn_bytes, dn_s=dn_scales,
        ))
    res = run_bass_kernel_spmd(nc, in_maps, core_ids=list(range(8)))

    inv = np.argsort(PI_D)
    out = np.empty((4, SEQ, D), np.float32)
    for c in range(8):
        b, role = c // 2, c % 2
        q0, qown = ROLE_CFG[role]["q0"], ROLE_CFG[role]["qown"]
        shard = np.asarray(res.results[c]["outp"])[:, :qown]
        out[b, q0:q0 + qown, :] = shard[inv].T
    return out
